# version 04 (final)
# baseline (speedup 1.0000x reference)
import numpy as np
import ml_dtypes
import concourse.bass as bass
import concourse.mybir as mybir
from concourse.bass_utils import run_bass_kernel_spmd

F32 = mybir.dt.float32
BF16 = mybir.dt.bfloat16
AF = mybir.ActivationFunctionType
ALU = mybir.AluOpType
AX = mybir.AxisListType.X

D = 1024
S = 2048
NSEQ = 4
TOK = S * NSEQ
HID = 2048
E = 8
NH = 16
DK = 64
NC8 = 8
NHC = 16
NST = 16
NTT = 4
NTC = 16
NG = 8
NW = 2
W = 1024

DBG_PHASES = None
DBG_DUMP = False
DBG_NSEQ = None
DBG_B_UPTO = None
DBG_B_LIMIT = None


def build_program():
    nc = bass.Bass("TRN2", target_bir_lowering=False)
    XB = nc.dram_tensor("XB", [TOK, D], F32, kind="ExternalInput").ap()
    WQ = nc.dram_tensor("WQ", [D, D], F32, kind="ExternalInput").ap()
    WKV = nc.dram_tensor("WKV", [D, 128], F32, kind="ExternalInput").ap()
    WO = nc.dram_tensor("WO", [D, D], F32, kind="ExternalInput").ap()
    SSD = nc.dram_tensor("SSD", [1, 1], F32, kind="ExternalInput").ap()
    RW = nc.dram_tensor("RW", [D, E], F32, kind="ExternalInput").ap()
    RSC = nc.dram_tensor("RSC", [E, 128], F32, kind="ExternalInput").ap()
    PESD = nc.dram_tensor("PESD", [1, E], F32, kind="ExternalInput").ap()
    G0D = nc.dram_tensor("G0D", [E, D, HID], F32, kind="ExternalInput").ap()
    G1D = nc.dram_tensor("G1D", [E, D, HID], F32, kind="ExternalInput").ap()
    LND = nc.dram_tensor("LND", [E, HID, D], F32, kind="ExternalInput").ap()
    OUT = nc.dram_tensor("OUT", [TOK, D], F32, kind="ExternalOutput").ap()
    _ik = dict(kind="ExternalOutput") if DBG_DUMP else {}
    ACC = nc.dram_tensor("ACC", [D, TOK], F32, **_ik).ap()
    CED = nc.dram_tensor("CED", [E, TOK], F32, **_ik).ap()
    XND = nc.dram_tensor("XND", [128, NC8 * S], BF16, **_ik).ap() if DBG_DUMP else None
    CDBG = nc.dram_tensor("CDBG", [128, 592], F32, **_ik).ap() if DBG_DUMP else None

    from contextlib import ExitStack
    with ExitStack() as stack:
        ec = stack.enter_context
        xnb = ec(nc.sbuf_tensor("xnb", [128, NC8 * S], BF16))
        bA = ec(nc.sbuf_tensor("bA", [128, 16384], BF16))
        bB = ec(nc.sbuf_tensor("bB", [128, 16384], BF16))
        bC = ec(nc.sbuf_tensor("bC", [128, 16384], BF16))
        bD = ec(nc.sbuf_tensor("bD", [128, 16384], BF16))
        fA = ec(nc.sbuf_tensor("fA", [128, 11264], F32))
        ident32 = ec(nc.sbuf_tensor("ident32", [128, 128], F32))
        ident16 = ec(nc.sbuf_tensor("ident16", [128, 128], BF16))
        ones1f = ec(nc.sbuf_tensor("ones1f", [1, 128], F32))
        ones128f = ec(nc.sbuf_tensor("ones128f", [128, 1], F32))
        ones64b = ec(nc.sbuf_tensor("ones64b", [128, 1], BF16))
        rwb = ec(nc.sbuf_tensor("rwb", [128, NC8 * E], F32))
        rwp = ec(nc.sbuf_tensor("rwp", [128, NC8 * E], F32))
        rs_cols = ec(nc.sbuf_tensor("rs_cols", [128, 8], F32))
        negcs = ec(nc.sbuf_tensor("negcs", [8, 1], F32))
        pes_sb = ec(nc.sbuf_tensor("pes_sb", [1, 8], F32))
        pes_bc = ec(nc.sbuf_tensor("pes_bc", [128, 8], F32))
        ss_sb = ec(nc.sbuf_tensor("ss_sb", [1, 1], F32))
        ss_col = ec(nc.sbuf_tensor("ss_col", [128, 1], F32))
        nqm = ec(nc.sbuf_tensor("nqm", [1, 16], F32))
        nkm = ec(nc.sbuf_tensor("nkm", [1, 1], F32))
        negm_row = ec(nc.sbuf_tensor("negm_row", [1, 16], F32))
        negm_cols = ec(nc.sbuf_tensor("negm_cols", [128, 16], F32))
        csr = ec(nc.sbuf_tensor("csr", [1, 8], F32))
        PS = ec(nc.psum_tensor("PS", [128, 3584], F32))
        PT = ec(nc.psum_tensor("PT", [128, 1024], BF16))
        SEM = {}
        for sname in ["pe", "sc", "ve", "gs", "ds", "st", "os", "w", "acc", "vg"]:
            SEM[sname] = ec(nc.semaphore(sname))

        def bank(b, w=512, p0=0, p1=128):
            return PS[p0:p1, b * 512:b * 512 + w]

        qt = [bA[:, c * 2048:(c + 1) * 2048] for c in range(NC8)]
        wqb = bB[:, 0:8192]
        sqv = [bB[:, 8192 + i * 2048:8192 + (i + 1) * 2048] for i in range(2)]
        attn_t = [[bB[:, 12288 + (2 * j + p) * 1024:12288 + (2 * j + p + 1) * 1024]
                   for p in range(2)] for j in range(2)]
        kt = bC[:, 0:2048]
        vt = bC[0:64, 2048:4096]
        vsb = bC[:, 4096:5136]
        wkvb = bC[:, 5136:6160]
        wob = bC[:, 6160:14352]
        att = [bD[:, c * 2048:(c + 1) * 2048] for c in range(NC8)]
        g0h = [bA[:, s_ * 8192:(s_ + 1) * 8192] for s_ in range(2)]
        g1h = [bB[:, s_ * 8192:(s_ + 1) * 8192] for s_ in range(2)]
        linb = bC[:, 0:16384]
        act = [bD[:, h * 1024:(h + 1) * 1024] for h in range(NHC)]
        x_tok = [fA[:, 0:1024], fA[:, 1024:2048]]
        x_fm = [fA[:, 2048 + dc * 512:2048 + (dc + 1) * 512] for dc in range(NC8)]
        bc_m = fA[:, 6144:6656]
        bc_rs = fA[:, 6656:7168]
        rowv = [fA[0:1, 7168 + i * 512:7168 + (i + 1) * 512] for i in range(4)]
        lg = fA[0:8, 9216:9728]
        t8a = fA[:, 9728:9736]
        t8b = fA[:, 9736:9744]
        exv = fA[:, 9744:9752]
        gwx = fA[:, 9752:9760]
        comb = fA[:, 9760:9768]
        colv = [fA[:, 9768 + i:9769 + i] for i in range(6)]
        ce_stage = fA[0:1, 10240:11264]
        tmp_wo = [fA[:, 0 + i * 512:512 + i * 512] for i in range(2)]
        bc_r = fA[:, 1024:1536]
        r_row = fA[0:1, 1536:2048]
        geluv = fA[:, 0:1024]
        tmpc = [fA[:, 1024 + i * 1024:2048 + i * 1024] for i in range(2)]
        bc_ce = [fA[:, 3072 + i * 1024:4096 + i * 1024] for i in range(2)]
        ce_row = [fA[0:1, 5120 + i * 1024:6144 + i * 1024] for i in range(2)]
        accf = [fA[:, 2048 + i * 1024:3072 + i * 1024] for i in range(2)]
        out_sb = [fA[:, 4096 + i * 1024:5120 + i * 1024] for i in range(2)]

        T, SS_, V, G, Y = [], [], [], [], []
        M = {}
        n_shadow = {}
        n = {"pe": 0, "sc": 0, "ve": 0, "gs": 0, "ds": 0, "st": 0, "os": 0, "w": 0, "acc": 0, "vg": 0}

        def mark(sem, key, dma=False):
            n[sem] += 16 if dma else 1
            M[(sem, key)] = n[sem]

        missing_keys = []

        def mget(sem, key):
            if (sem, key) not in M:
                if DBG_PHASES is not None:
                    missing_keys.append((sem, key))
                    return 0
                raise KeyError((sem, key))
            return M[(sem, key)]

        def wt(lst, sem, key):
            lst.append(lambda e: e.wait_ge(SEM[sem], M[(sem, key)]))

        def wt_val(lst, sem, val):
            if val > 0:
                lst.append(lambda e: e.wait_ge(SEM[sem], val))

        def premark(sem, key):
            mark(sem, key, dma=True)

        def vgv():
            n["vg"] += 1
            return n["vg"]

        def gv(v, inst, val):
            inst.then_inc(SEM["vg"], 1)
            v.wait_ge(SEM["vg"], val)

        def emit(lst, fn, sem=None, key=None, dma=False):
            if sem is not None:
                mark(sem, key, dma)
                amt = 16 if dma else 1
                def f(e, fn=fn, sem=sem, amt=amt):
                    last = fn(e)
                    last.then_inc(SEM[sem], amt)
                lst.append(f)
            else:
                lst.append(fn)

        def join_all():
            snap = dict(n)
            for lst, own in ((T, "pe"), (SS_, "sc"), (V, "ve"), (G, None), (Y, None)):
                for sname, v in snap.items():
                    if sname == own or v == 0:
                        continue
                    if sname == "gs" and own in ("pe", "sc", "ve") and lst is not G:
                        pass
                    wt_val(lst, sname, v)

        def prep():
            premark("w", "prwb")
            premark("w", "prss")
            def gf(g):
                g.dma_start(rwb[:].rearrange("p (c e) -> p c e", c=NC8),
                            RW.rearrange("(c p) e -> p c e", p=128)).then_inc(SEM["w"], 16)
                g.dma_start(ss_sb[:], SSD).then_inc(SEM["w"], 16)
                g.memset(ones1f[:], 1.0)
                g.memset(ones128f[:], 1.0)
                g.memset(ones64b[:], 1.0)
                g.memset(ident32[:], 0.0)
                g.memset(ident16[:], 0.0)
                g.affine_select(out=ident32[:], in_=ident32[:], compare_op=ALU.not_equal,
                                fill=1.0, base=0, pattern=[[-1, 128]], channel_multiplier=1)
                g.affine_select(out=ident16[:], in_=ident16[:], compare_op=ALU.not_equal,
                                fill=1.0, base=0, pattern=[[-1, 128]], channel_multiplier=1)
                for tc in range(NTC):
                    g.memset(vsb[:, tc * 65 + 64:tc * 65 + 65], 1.0)
                last = g.memset(fA[:, 0:1], 0.0)
                return last
            emit(G, gf, "gs", "prep")
            def gf2(g):
                g.dma_start(pes_sb[:], PESD).then_inc(SEM["w"], 16)
                return g.dma_start(fA[0:8, 0:128], RSC)
            premark("w", "prpes")
            emit(G, gf2, "w", "rsc", dma=True)
            wt(T, "w", "rsc")
            wt(T, "gs", "prep")
            emit(T, lambda t: t.transpose(PS[:, 0:8], fA[0:8, 0:128], ident32[0:8, 0:8]),
                 "pe", "rscT")
            wt(V, "pe", "rscT")
            wt(V, "w", "rsc")
            emit(V, lambda v: v.tensor_copy(rs_cols[:], PS[:, 0:8]), "ve", "rsc")
            def vf(v):
                v.wait_ge(SEM["ve"], mget("ve", "rsc"))
                last = None
                for dc in range(NC8):
                    last = v.tensor_scalar(rwp[:, dc * E:(dc + 1) * E], rwb[:, dc * E:(dc + 1) * E],
                                           rs_cols[:, dc:dc + 1], None, ALU.mult)
                return last
            emit(V, vf, "ve", "rwp")
            def tf(t):
                t.wait_ge(SEM["ve"], mget("ve", "rwp"))
                last = None
                for dc in range(NC8):
                    last = t.matmul(PS[0:1, 512:520], ones128f[:], rwp[:, dc * E:(dc + 1) * E],
                                    start=(dc == 0), stop=(dc == NC8 - 1))
                return last
            emit(T, tf, "pe", "cs")
            wt(V, "pe", "cs")
            emit(V, lambda v: v.tensor_copy(csr[:], PS[0:1, 512:520]), "ve", "csr")
            wt(T, "ve", "csr")
            emit(T, lambda t: t.transpose(PS[:, 1024:1032][0:8, 0:1], csr[:], ident32[0:1, 0:1]),
                 "pe", "csT")
            wt(V, "pe", "csT")
            emit(V, lambda v: v.tensor_scalar(negcs[:], PS[0:8, 1024:1025], -1.0, None, ALU.mult),
                 "ve", "negcs")
            def tf2(t):
                t.wait_ge(SEM["w"], mget("w", "rsc"))
                t.matmul(PS[:, 1536:1544], ones1f[:], pes_sb[:], start=True, stop=True)
                return t.matmul(PS[:, 2048:2049], ones1f[:], ss_sb[:], start=True, stop=True)
            emit(T, tf2, "pe", "pes")
            wt(V, "pe", "pes")
            def vf2(v):
                v.tensor_copy(pes_bc[:], PS[:, 1536:1544])
                return v.tensor_copy(ss_col[:], PS[:, 2048:2049])
            emit(V, vf2, "ve", "pes")

        def phase_a(s):
            base = s * S
            for st in range(NST):
                tt = st // 4
                k4 = st % 4
                def yf(y, st=st):
                    return y.dma_start(x_tok[st % 2][:], XB[base + st * 128:base + (st + 1) * 128, :])
                if st >= 2:
                    wt(Y, "pe", ("axT", s, st - 2))
                emit(Y, yf, "ds", ("ax", s, st), dma=True)
                def tf(t, st=st, s=s):
                    t.wait_ge(SEM["ds"], mget("ds", ("ax", s, st)))
                    last = None
                    for dc in range(NC8):
                        last = t.transpose(PS[:, (dc // 4) * 512 + (dc % 4) * 128:(dc // 4) * 512 + (dc % 4) * 128 + 128],
                                           x_tok[st % 2][:, dc * 128:(dc + 1) * 128], ident32[:])
                    return last
                if st >= 1:
                    wt(T, "ve", ("axev", s, st - 1))
                emit(T, tf, "pe", ("axT", s, st))
                def vf(v, st=st, s=s, k4=k4):
                    v.wait_ge(SEM["pe"], mget("pe", ("axT", s, st)))
                    last = None
                    for dc in range(NC8):
                        last = v.tensor_copy(x_fm[dc][:, k4 * 128:(k4 + 1) * 128],
                                             PS[:, (dc // 4) * 512 + (dc % 4) * 128:(dc // 4) * 512 + (dc % 4) * 128 + 128])
                    return last
                emit(V, vf, "ve", ("axev", s, st))
                if k4 < 3:
                    continue
                sqx = bc_rs
                for dc in range(NC8):
                    def sf(s_, dc=dc, s=s, st=st):
                        s_.wait_ge(SEM["ve"], mget("ve", ("axev", s, st)))
                        if dc > 0:
                            s_.wait_ge(SEM["pe"], mget("pe", ("asq", s, st, dc - 1)))
                        return s_.activation(sqx[:], x_fm[dc][:], AF.Square)
                    emit(SS_, sf, "sc", ("asq", s, st, dc))
                    def tf2(t, dc=dc, s=s, st=st):
                        if dc == 0:
                            t.wait_ge(SEM["ve"], mget("ve", ("axev", s, st)))
                            if st >= 4:
                                t.wait_ge(SEM["ve"], mget("ve", ("acest", s, st - 4)))
                        t.matmul(PS[0:1, 2 * 512:2 * 512 + 512], ones128f[:], x_fm[dc][:],
                                 start=(dc == 0), stop=(dc == NC8 - 1))
                        t.wait_ge(SEM["sc"], mget("sc", ("asq", s, st, dc)))
                        return t.matmul(PS[0:1, 3 * 512:3 * 512 + 512], ones128f[:], sqx[:],
                                        start=(dc == 0), stop=(dc == NC8 - 1))
                    emit(T, tf2, "pe", ("asq", s, st, dc))
                gvv = [vgv() for _ in range(4)]
                def vf2(v, s=s, st=st, gvv=gvv):
                    v.wait_ge(SEM["pe"], mget("pe", ("asq", s, st, NC8 - 1)))
                    gv(v, v.tensor_scalar(rowv[0][:], PS[0:1, 2 * 512:2 * 512 + 512], 1.0 / D, None, ALU.mult), gvv[0])
                    gv(v, v.tensor_scalar(rowv[1][:], PS[0:1, 3 * 512:3 * 512 + 512], 1.0 / D, None, ALU.mult), gvv[1])
                    gv(v, v.tensor_tensor(rowv[2][:], rowv[0][:], rowv[0][:], ALU.mult), gvv[2])
                    gv(v, v.tensor_tensor(rowv[1][:], rowv[1][:], rowv[2][:], ALU.subtract), gvv[3])
                    return v.tensor_scalar(rowv[2][:], rowv[1][:], 1e-5, None, ALU.add)
                emit(V, vf2, "ve", ("arow1", s, st))
                def sf2(s_, s=s, st=st):
                    s_.wait_ge(SEM["ve"], mget("ve", ("arow1", s, st)))
                    return s_.activation(rowv[3][:], rowv[2][:], AF.Sqrt)
                emit(SS_, sf2, "sc", ("arow", s, st))
                gvv3 = [vgv() for _ in range(3)]
                def vf3(v, s=s, st=st, gvv3=gvv3):
                    v.wait_ge(SEM["sc"], mget("sc", ("arow", s, st)))
                    gv(v, v.reciprocal(rowv[3][:], rowv[3][:]), gvv3[0])
                    gv(v, v.tensor_tensor(rowv[2][:], rowv[3][:], rowv[3][:], ALU.mult), gvv3[1])
                    gv(v, v.tensor_tensor(rowv[2][:], rowv[1][:], rowv[2][:], ALU.mult), gvv3[2])
                    return v.tensor_scalar(rowv[2][:], rowv[2][:], 1e-6, None, ALU.add)
                emit(V, vf3, "ve", ("arow2", s, st))
                def sf3(s_, s=s, st=st):
                    s_.wait_ge(SEM["ve"], mget("ve", ("arow2", s, st)))
                    return s_.activation(rowv[2][:], rowv[2][:], AF.Sqrt)
                emit(SS_, sf3, "sc", ("arow3", s, st))
                gvv4 = [vgv() for _ in range(2)]
                def vf4(v, s=s, st=st, gvv4=gvv4):
                    v.wait_ge(SEM["sc"], mget("sc", ("arow3", s, st)))
                    gv(v, v.reciprocal(rowv[2][:], rowv[2][:]), gvv4[0])
                    gv(v, v.tensor_scalar(rowv[2][:], rowv[2][:], 0.03125, None, ALU.mult), gvv4[1])
                    return v.tensor_tensor(rowv[2][:], rowv[3][:], rowv[2][:], ALU.mult)
                emit(V, vf4, "ve", ("arow3", s, st))
                def tf3(t, s=s, st=st):
                    t.wait_ge(SEM["ve"], mget("ve", ("arow3", s, st)))
                    t.matmul(PS[:, 4 * 512:4 * 512 + 512], ones1f[:], rowv[0][:], start=True, stop=True)
                    return t.matmul(PS[:, 5 * 512:5 * 512 + 512], ones1f[:], rowv[3][:], start=True, stop=True)
                emit(T, tf3, "pe", ("abc", s, st))
                def vf5(v, s=s, st=st):
                    v.wait_ge(SEM["pe"], mget("pe", ("abc", s, st)))
                    v.tensor_copy(bc_m[:], PS[:, 4 * 512:4 * 512 + 512])
                    return v.tensor_copy(bc_rs[:], PS[:, 5 * 512:5 * 512 + 512])
                emit(V, vf5, "ve", ("abc", s, st))
                def tfr(t, s=s, st=st):
                    last = None
                    for dc in range(NC8):
                        last = t.matmul(PS[0:8, 6 * 512:6 * 512 + 512], rwp[:, dc * E:(dc + 1) * E],
                                        x_fm[dc][:], start=(dc == 0), stop=(dc == NC8 - 1))
                    return last
                emit(T, tfr, "pe", ("art", s, st))
                def tfr2(t, s=s, st=st):
                    t.wait_ge(SEM["ve"], mget("ve", ("abc", s, st)))
                    return t.matmul(PS[:, 4 * 512:4 * 512 + 512], ones1f[:], rowv[2][:], start=True, stop=True)
                emit(T, tfr2, "pe", ("arr", s, st))
                gvv6 = [vgv() for _ in range(NC8)]
                def vf6(v, s=s, st=st, tt=tt, gvv6=gvv6):
                    v.wait_ge(SEM["pe"], mget("pe", ("art", s, st)))
                    last = None
                    for dc in range(NC8):
                        gv(v, v.tensor_tensor(x_fm[dc][:], x_fm[dc][:], bc_m[:], ALU.subtract), gvv6[dc])
                        last = v.tensor_tensor(xnb[:, dc * S + tt * 512:dc * S + (tt + 1) * 512],
                                               x_fm[dc][:], bc_rs[:], ALU.mult)
                    return last
                emit(V, vf6, "ve", ("aln", s, st))
                gvv7 = vgv()
                def vf7(v, s=s, st=st, gvv7=gvv7):
                    v.wait_ge(SEM["pe"], mget("pe", ("art", s, st)))
                    gv(v, v.scalar_tensor_tensor(lg[:], bc_m[0:8, :], negcs[0:8, 0:1],
                                                 PS[0:8, 6 * 512:6 * 512 + 512], ALU.mult, ALU.add), gvv7)
                    v.wait_ge(SEM["pe"], mget("pe", ("arr", s, st)))
                    return v.tensor_tensor(lg[:], lg[:], PS[0:8, 4 * 512:4 * 512 + 512], ALU.mult)
                emit(V, vf7, "ve", ("alg", s, st))
                def tf4(t, s=s, st=st):
                    t.wait_ge(SEM["ve"], mget("ve", ("alg", s, st)))
                    last = None
                    for k in range(4):
                        last = t.transpose(PS[:, 5 * 512 + k * 8:5 * 512 + (k + 1) * 8],
                                           lg[0:8, k * 128:(k + 1) * 128], ident32[0:8, 0:8])
                    return last
                emit(T, tf4, "pe", ("algT", s, st))
                for k in range(4):
                    stk = tt * 4 + k
                    gv8 = [vgv() for _ in range(4)]
                    def vf8(v, s=s, st=st, k=k, stk=stk, gv8=gv8):
                        v.wait_ge(SEM["pe"], mget("pe", ("algT", s, st)))
                        pl = PS[:, 5 * 512 + k * 8:5 * 512 + (k + 1) * 8]
                        gv(v, v.reduce_max(colv[0][:], pl, AX), gv8[0])
                        gv(v, v.tensor_scalar(t8a[:], pl, colv[0][:], None, ALU.is_ge), gv8[1])
                        gv(v, v.scalar_tensor_tensor(t8b[:], t8a[:], -1e30, pl, ALU.mult, ALU.add), gv8[2])
                        gv(v, v.reduce_max(colv[1][:], t8b[:], AX), gv8[3])
                        v.tensor_scalar(t8b[:], pl, colv[1][:], None, ALU.is_ge)
                        return v.tensor_scalar(colv[2][:], colv[0][:], -1.0, None, ALU.mult)
                    emit(V, vf8, "ve", ("acmb", s, stk))
                    def sf4(s_, s=s, st=st, k=k, stk=stk):
                        s_.wait_ge(SEM["ve"], mget("ve", ("acmb", s, stk)))
                        pl = PS[:, 5 * 512 + k * 8:5 * 512 + (k + 1) * 8]
                        return s_.activation(exv[:], pl, AF.Exp, bias=colv[2][:, 0:1])
                    emit(SS_, sf4, "sc", ("aex", s, stk))
                    gv9 = [vgv() for _ in range(4)]
                    def vf9(v, s=s, st=st, k=k, stk=stk, gv9=gv9):
                        v.wait_ge(SEM["sc"], mget("sc", ("aex", s, stk)))
                        if stk >= 1:
                            v.wait_ge(SEM["pe"], mget("pe", ("aceT", s, stk - 1)))
                        gv(v, v.tensor_tensor(gwx[:], exv[:], t8b[:], ALU.mult), gv9[0])
                        gv(v, v.reduce_sum(colv[3][:], gwx[:], AX), gv9[1])
                        gv(v, v.reciprocal(colv[4][:], colv[3][:]), gv9[2])
                        gv(v, v.tensor_scalar(comb[:], gwx[:], colv[4][:, 0:1], None, ALU.mult), gv9[3])
                        return v.tensor_tensor(comb[:], comb[:], pes_bc[:], ALU.mult)
                    emit(V, vf9, "ve", ("acomb", s, stk))
                    if DBG_DUMP and s == 0 and stk == 0:
                        def vdbg(v, k=k):
                            pl = PS[:, 5 * 512 + k * 8:5 * 512 + (k + 1) * 8]
                            return v.tensor_copy(fA[:, 9776:9784], pl)
                        emit(V, vdbg, "ve", ("adbg", s, stk))
                        def ydbg(y):
                            y.wait_ge(SEM["ve"], mget("ve", ("adbg", 0, 0)))
                            return y.dma_start(CDBG, fA[:, 9216:9808])
                        emit(Y, ydbg, "os", ("adbg",), dma=True)
                        def vfreeze(v):
                            return v.tensor_copy(fA[:, 9806:9808], fA[:, 9806:9808])
                        def vwaitdump(v):
                            v.wait_ge(SEM["os"], mget("os", ("adbg",)))
                            return vfreeze(v)
                        emit(V, vwaitdump, "ve", ("adbgw",))
                    def tf5(t, s=s, stk=stk):
                        t.wait_ge(SEM["ve"], mget("ve", ("acomb", s, stk)))
                        last = None
                        for e in range(E):
                            last = t.transpose(PS[0:1, 2 * 512 + e * 128:2 * 512 + (e + 1) * 128],
                                               comb[:, e:e + 1], ident32[:])
                        return last
                    emit(T, tf5, "pe", ("aceT", s, stk))
                    def vf10(v, s=s, stk=stk):
                        v.wait_ge(SEM["pe"], mget("pe", ("aceT", s, stk)))
                        if stk >= 1:
                            v.wait_ge(SEM["st"], mget("st", ("ace", s, stk - 1)))
                        return v.tensor_copy(ce_stage[:], PS[0:1, 2 * 512:2 * 512 + 1024])
                    emit(V, vf10, "ve", ("acest", s, stk))
                    def yf2(y, s=s, stk=stk):
                        y.wait_ge(SEM["ve"], mget("ve", ("acest", s, stk)))
                        dst = bass.AP(CED.tensor, (s * S + stk * 128),
                                      [[TOK, 1], [TOK, 8], [1, 128]])
                        return y.dma_start(dst, ce_stage[:].rearrange("p (e t) -> p e t", e=8))
                    emit(Y, yf2, "st", ("ace", s, stk), dma=True)

        def phase_b(s):
            premark("w", ("bwq", s))
            premark("w", ("bwkv", s))
            def gw(g, s=s):
                g.dma_start(wqb[:].rearrange("p (c o) -> p c o", c=NC8),
                            WQ.rearrange("(c p) o -> p c o", p=128)).then_inc(SEM["w"], 16)
                g.dma_start(wkvb[:, 0:1024].rearrange("p (c o) -> p c o", c=NC8),
                            WKV.rearrange("(c p) o -> p c o", p=128)).then_inc(SEM["w"], 16)
                return g.dma_start(wob[:].rearrange("p (c o) -> p c o", c=NC8),
                                   WO.rearrange("(c p) o -> p c o", p=128))
            emit(G, gw, "w", ("bw", s), dma=True)
            def gones(g, s=s):
                last = None
                for tc in range(NTC):
                    last = g.memset(vsb[:, tc * 65 + 64:tc * 65 + 65], 1.0)
                return last
            emit(G, gones, "gs", ("bones", s))
            wkv_v = [wkvb[:, dc * 128:(dc + 1) * 128] for dc in range(NC8)]
            wq_v = lambda dc, oc: wqb[:, dc * 1024 + oc * 128:dc * 1024 + (oc + 1) * 128]
            wo_v = lambda ac, oc: wob[:, ac * 1024 + oc * 128:ac * 1024 + (oc + 1) * 128]
            for oc in range(NC8):
                for th in range(2):
                    def tf(t, oc=oc, th=th, s=s):
                        if (oc, th) == (0, 0):
                            t.wait_ge(SEM["w"], mget("w", ("bw", s)))
                        if oc >= 1:
                            t.wait_ge(SEM["sc"], mget("sc", ("bqev", s, oc - 1, th)))
                        last = None
                        for dc in range(NC8):
                            for ts2 in range(2):
                                ts = th * 2 + ts2
                                last = t.matmul(bank(2 * th + ts2),
                                                wq_v(dc, oc),
                                                xnb[:, dc * S + ts * 512:dc * S + (ts + 1) * 512],
                                                start=(dc == 0), stop=(dc == NC8 - 1))
                        return last
                    emit(T, tf, "pe", ("bq", s, oc, th))
                    def sf(s_, oc=oc, th=th, s=s):
                        s_.wait_ge(SEM["pe"], mget("pe", ("bq", s, oc, th)))
                        return s_.activation(qt[oc][:, th * 1024:(th + 1) * 1024],
                                             PS[:, (2 * th) * 512:(2 * th) * 512 + 1024], AF.Copy)
                    emit(SS_, sf, "sc", ("bqev", s, oc, th))
            for th in range(2):
                def tf2(t, th=th, s=s):
                    if th == 1:
                        t.wait_ge(SEM["ve"], mget("ve", ("bkv", s, 0)))
                    last = None
                    for dc in range(NC8):
                        for ts2 in range(2):
                            ts = th * 2 + ts2
                            last = t.matmul(bank(4 + ts2),
                                            wkv_v[dc],
                                            xnb[:, dc * S + ts * 512:dc * S + (ts + 1) * 512],
                                            start=(dc == 0), stop=(dc == NC8 - 1))
                    return last
                emit(T, tf2, "pe", ("bkv", s, th))
                def vf(v, th=th, s=s):
                    v.wait_ge(SEM["pe"], mget("pe", ("bkv", s, th)))
                    v.tensor_copy(kt[0:64, th * 1024:(th + 1) * 1024], PS[0:64, 4 * 512:4 * 512 + 1024])
                    v.tensor_copy(kt[64:128, th * 1024:(th + 1) * 1024], PS[0:64, 4 * 512:4 * 512 + 1024])
                    return v.tensor_copy(vt[:, th * 1024:(th + 1) * 1024], PS[64:128, 4 * 512:4 * 512 + 1024])
                emit(V, vf, "ve", ("bkv", s, th))
            for tc in range(NTC):
                def tf3(t, tc=tc, s=s):
                    if tc == 0:
                        t.wait_ge(SEM["ve"], mget("ve", ("bkv", s, 1)))
                    if tc >= 2:
                        t.wait_ge(SEM["ve"], mget("ve", ("bvs", s, tc - 2)))
                    return t.transpose(PT[:, (tc % 2) * 512:(tc % 2) * 512 + 64],
                                       vt[:, tc * 128:(tc + 1) * 128], ident16[0:64, 0:64])
                emit(T, tf3, "pe", ("bvt", s, tc))
                def vf2(v, tc=tc, s=s):
                    v.wait_ge(SEM["pe"], mget("pe", ("bvt", s, tc)))
                    return v.tensor_copy(vsb[:, tc * 65:tc * 65 + 64],
                                         PT[:, (tc % 2) * 512:(tc % 2) * 512 + 64])
                emit(V, vf2, "ve", ("bvs", s, tc))
            for oc in range(NC8):
                def sf2(s_, oc=oc, s=s):
                    if oc >= 2:
                        s_.wait_ge(SEM["pe"], mget("pe", ("bnqm", s, oc - 2, 1)))
                    return s_.activation(sqv[oc % 2][:], qt[oc][:], AF.Square)
                emit(SS_, sf2, "sc", ("bsq", s, oc))
                for j in range(2):
                    def tf4(t, oc=oc, j=j, s=s):
                        t.wait_ge(SEM["sc"], mget("sc", ("bsq", s, oc)))
                        if (oc, j) != (0, 0):
                            pj = (oc, j - 1) if j else (oc - 1, 1)
                            t.wait_ge(SEM["ve"], mget("ve", ("bnq", s) + pj))
                        else:
                            t.wait_ge(SEM["sc"], mget("sc", ("bqev", s, NC8 - 1, 1)))
                            t.wait_ge(SEM["ve"], mget("ve", ("bkv", s, 1)))
                        last = None
                        for ts in range(4):
                            last = t.matmul(PS[0:1, (2 + ts) * 512:(2 + ts) * 512 + 512],
                                            ones64b[j * 64:(j + 1) * 64, 0:1],
                                            sqv[oc % 2][j * 64:(j + 1) * 64, ts * 512:(ts + 1) * 512],
                                            start=True, stop=True, tile_position=(j * 64, 0))
                        return last
                    emit(T, tf4, "pe", ("bnqm", s, oc, j))
                    def vf3(v, oc=oc, j=j, s=s):
                        v.wait_ge(SEM["pe"], mget("pe", ("bnqm", s, oc, j)))
                        return v.reduce_max(nqm[0:1, 2 * oc + j:2 * oc + j + 1],
                                            PS[0:1, 2 * 512:2 * 512 + 2048], AX)
                    emit(V, vf3, "ve", ("bnq", s, oc, j))
            def sf3(s_, s=s):
                s_.wait_ge(SEM["pe"], mget("pe", ("bnqm", s, 6, 1)))
                return s_.activation(sqv[0][0:64, :], kt[0:64, :], AF.Square)
            emit(SS_, sf3, "sc", ("bsqk", s))
            def tf5(t, s=s):
                t.wait_ge(SEM["sc"], mget("sc", ("bsqk", s)))
                t.wait_ge(SEM["ve"], mget("ve", ("bnq", s, NC8 - 1, 1)))
                last = None
                for ts in range(4):
                    last = t.matmul(PS[0:1, (2 + ts) * 512:(2 + ts) * 512 + 512],
                                    ones64b[0:64, 0:1],
                                    sqv[0][0:64, ts * 512:(ts + 1) * 512],
                                    start=True, stop=True)
                return last
            emit(T, tf5, "pe", ("bnk", s))
            gvb = vgv()
            def vf4(v, s=s, gvb=gvb):
                v.wait_ge(SEM["pe"], mget("pe", ("bnk", s)))
                gv(v, v.reduce_max(nkm[0:1, 0:1], PS[0:1, 2 * 512:2 * 512 + 2048], AX), gvb)
                return v.tensor_single_scalar(nqm[:], nqm[:], nkm[0:1, 0:1], ALU.mult)
            emit(V, vf4, "ve", ("bt1", s))
            def sf4(s_, s=s):
                s_.wait_ge(SEM["ve"], mget("ve", ("bt1", s)))
                return s_.activation(nqm[:], nqm[:], AF.Sqrt)
            emit(SS_, sf4, "sc", ("bsqrt", s))
            def vf5(v, s=s):
                v.wait_ge(SEM["sc"], mget("sc", ("bsqrt", s)))
                return v.tensor_scalar(negm_row[:], nqm[:], ss_sb[0:1, 0:1], -1.0, ALU.mult, ALU.mult)
            emit(V, vf5, "ve", ("bnegm", s))
            def tf6(t, s=s):
                t.wait_ge(SEM["ve"], mget("ve", ("bnegm", s)))
                return t.matmul(PS[:, 0:16], ones1f[:], negm_row[:], start=True, stop=True)
            emit(T, tf6, "pe", ("bnegm", s))
            def vf6(v, s=s):
                v.wait_ge(SEM["pe"], mget("pe", ("bnegm", s)))
                return v.tensor_copy(negm_cols[:], PS[:, 0:16])
            emit(V, vf6, "ve", ("bnegmc", s))
            if DBG_B_UPTO == "norms":
                if DBG_DUMP:
                    def ydn(y, s=s):
                        y.wait_ge(SEM["ve"], mget("ve", ("bnegmc", s)))
                        return y.dma_start(CDBG[0:128, 0:16], negm_cols[:])
                    emit(Y, ydn, "os", ("bnorm", s), dma=True)
                return
            blim = DBG_B_LIMIT if DBG_B_LIMIT is not None else 32
            for si in range(4):
                if si * NC8 >= blim:
                    break
                for c in range(NC8):
                    if si * NC8 + c >= blim:
                        break
                    for g in range(NG):
                        for j in range(2):
                            def tf7(t, s=s, si=si, c=c, g=g, j=j):
                                if (si, c, g) == (0, 0, 0) and j == 0:
                                    t.wait_ge(SEM["ve"], mget("ve", ("bnegmc", s)))
                                if c == 0 and g == 0 and si > 0:
                                    t.wait_ge(SEM["ve"], mget("ve", ("bwo", s, si - 1, NC8 - 1)))
                                pg = None
                                if g > 0:
                                    pg = (si, c, g - 1, j)
                                elif (si, c) != (0, 0):
                                    pc = (si, c - 1) if c > 0 else (si - 1, NC8 - 1)
                                    pg = (pc[0], pc[1], NG - 1, j)
                                if pg is not None:
                                    t.wait_ge(SEM["sc"], mget("sc", ("bex", s) + pg))
                                last = None
                                for tc2 in range(2):
                                    tc = 2 * g + tc2
                                    last = t.matmul(bank(2 * j + tc2),
                                                    kt[j * 64:(j + 1) * 64, tc * 128:(tc + 1) * 128],
                                                    qt[c][j * 64:(j + 1) * 64, si * 512:(si + 1) * 512],
                                                    start=True, stop=True, tile_position=(j * 64, 0))
                                return last
                            emit(T, tf7, "pe", ("bsc", s, si, c, g, j))
                            def sf5(s_, s=s, si=si, c=c, g=g, j=j):
                                s_.wait_ge(SEM["pe"], mget("pe", ("bsc", s, si, c, g, j)))
                                if g >= 2:
                                    s_.wait_ge(SEM["pe"], mget("pe", ("bavg", s, si, c, g - 2)))
                                elif (si, c) != (0, 0):
                                    pc = (si, c - 1) if c > 0 else (si - 1, NC8 - 1)
                                    s_.wait_ge(SEM["pe"], mget("pe", ("bavg", s, pc[0], pc[1], NG - 2 + g)))
                                h = 2 * c + j
                                return s_.activation(attn_t[j][g % 2][:],
                                                     PS[:, (2 * j) * 512:(2 * j) * 512 + 1024],
                                                     AF.Exp, bias=negm_cols[:, h:h + 1],
                                                     scale=ss_col[:, 0:1])
                            emit(SS_, sf5, "sc", ("bex", s, si, c, g, j))
                        def tf8(t, s=s, si=si, c=c, g=g):
                            if (si, c, g) == (0, 0, 0):
                                t.wait_ge(SEM["gs"], mget("gs", ("bones", s)))
                            last = None
                            for j in range(2):
                                t.wait_ge(SEM["sc"], mget("sc", ("bex", s, si, c, g, j)))
                                if g == 0 and (si, c) != (0, 0):
                                    pc = (si, c - 1) if c > 0 else (si - 1, NC8 - 1)
                                    t.wait_ge(SEM["ve"], mget("ve", ("batt", s, pc[0], pc[1], j)))
                                for tc2 in range(2):
                                    tc = 2 * g + tc2
                                    last = t.matmul(bank(4 + j, p1=65)[0:65, 0:512],
                                                    vsb[:, tc * 65:tc * 65 + 65],
                                                    attn_t[j][g % 2][:, tc2 * 512:(tc2 + 1) * 512],
                                                    start=(tc == 0), stop=(tc == NTC - 1))
                            return last
                        emit(T, tf8, "pe", ("bavg", s, si, c, g))
                    for j in range(2):
                        def vf7(v, s=s, si=si, c=c, j=j):
                            v.wait_ge(SEM["pe"], mget("pe", ("bavg", s, si, c, NG - 1)))
                            return v.reciprocal(r_row[:], PS[64:65, (4 + j) * 512:(4 + j) * 512 + 512])
                        emit(V, vf7, "ve", ("br", s, si, c, j))
                        def tf9(t, s=s, si=si, c=c, j=j):
                            t.wait_ge(SEM["ve"], mget("ve", ("br", s, si, c, j)))
                            return t.matmul(bank(6), ones1f[:], r_row[:], start=True, stop=True)
                        emit(T, tf9, "pe", ("bbcr", s, si, c, j))
                        gvbc = vgv()
                        def vf8(v, s=s, si=si, c=c, j=j, gvbc=gvbc):
                            v.wait_ge(SEM["pe"], mget("pe", ("bbcr", s, si, c, j)))
                            gv(v, v.tensor_copy(bc_r[:], PS[:, 6 * 512:6 * 512 + 512]), gvbc)
                            return v.tensor_tensor(att[c][j * 64:(j + 1) * 64, si * 512:(si + 1) * 512],
                                                   PS[0:64, (4 + j) * 512:(4 + j) * 512 + 512],
                                                   bc_r[0:64, :], ALU.mult)
                        emit(V, vf8, "ve", ("batt", s, si, c, j))
                for oc in range(NC8):
                    def tf10(t, s=s, si=si, oc=oc):
                        if oc == 0:
                            t.wait_ge(SEM["ve"], mget("ve", ("batt", s, si, NC8 - 1, 1)))
                            t.wait_ge(SEM["sc"], mget("sc", ("bex", s, si, NC8 - 1, NG - 1, 1)))
                        if oc >= 2:
                            t.wait_ge(SEM["ve"], mget("ve", ("bwo", s, si, oc - 2)))
                        last = None
                        for ac in range(NC8):
                            last = t.matmul(bank(oc % 2), wo_v(ac, oc),
                                            att[ac][:, si * 512:(si + 1) * 512],
                                            start=(ac == 0), stop=(ac == NC8 - 1))
                        return last
                    emit(T, tf10, "pe", ("bwo", s, si, oc))
                    def vf9(v, s=s, si=si, oc=oc):
                        v.wait_ge(SEM["pe"], mget("pe", ("bwo", s, si, oc)))
                        idx = si * NC8 + oc
                        if idx >= 2:
                            pi = idx - 2
                            v.wait_ge(SEM["st"], mget("st", ("bst", s, pi // NC8, pi % NC8)))
                        return v.tensor_copy(tmp_wo[oc % 2][:], bank(oc % 2))
                    emit(V, vf9, "ve", ("bwo", s, si, oc))
                    def yf(y, s=s, si=si, oc=oc):
                        y.wait_ge(SEM["ve"], mget("ve", ("bwo", s, si, oc)))
                        return y.dma_start(ACC[oc * 128:(oc + 1) * 128,
                                               s * S + si * 512:s * S + (si + 1) * 512],
                                           tmp_wo[oc % 2][:])
                    emit(Y, yf, "st", ("bst", s, si, oc), dma=True)

        def c_prev_hc(s, e, w, hh, hcq):
            idx = w * 16 + hh * 8 + hcq
            if idx > 0:
                pidx = idx - 1
                return (s, e, pidx // 16, (pidx % 16) // 8 * 8 + pidx % 8)
            if e > 0:
                return (s, e - 1, NW - 1, 15)
            if s > 0:
                return (s - 1, E - 1, NW - 1, 15)
            return None

        def phase_c(s):
            g0v = lambda sl, dc, hcq: g0h[sl][:, dc * 1024 + hcq * 128:dc * 1024 + (hcq + 1) * 128]
            g1v = lambda sl, dc, hcq: g1h[sl][:, dc * 1024 + hcq * 128:dc * 1024 + (hcq + 1) * 128]
            lnv = lambda hc, oc: linb[:, hc * 1024 + oc * 128:hc * 1024 + (oc + 1) * 128]
            for e in range(E):
                for hh in range(2):
                    def gw(g, s=s, e=e, hh=hh):
                        if e > 0:
                            g.wait_ge(SEM["pe"], mget("pe", ("cg", s, e - 1, 1, hh, 7)))
                        elif s > 0:
                            g.wait_ge(SEM["pe"], mget("pe", ("cg", s - 1, E - 1, 1, hh, 7)))
                        g.dma_start(g0h[hh][:].rearrange("p (c h) -> p c h", c=NC8),
                                    G0D[e][:, hh * 1024:(hh + 1) * 1024].rearrange("(c p) h -> p c h", p=128)
                                    ).then_inc(SEM["w"], 16)
                        return g.dma_start(g1h[hh][:].rearrange("p (c h) -> p c h", c=NC8),
                                           G1D[e][:, hh * 1024:(hh + 1) * 1024].rearrange("(c p) h -> p c h", p=128))
                    premark("w", ("cw0", s, e, hh))
                    emit(G, gw, "w", ("cw", s, e, hh), dma=True)
                def gl(g, s=s, e=e):
                    if e > 0:
                        g.wait_ge(SEM["pe"], mget("pe", ("cd", s, e - 1, 1, 7)))
                    elif s > 0:
                        g.wait_ge(SEM["pe"], mget("pe", ("cd", s - 1, E - 1, 1, 7)))
                    return g.dma_start(linb[:].rearrange("p (c o) -> p c o", c=NHC),
                                       LND[e].rearrange("(c p) o -> p c o", p=128))
                emit(G, gl, "w", ("cl", s, e), dma=True)
                for w in range(NW):
                    def yce(y, s=s, e=e, w=w):
                        if e > 0:
                            y.wait_ge(SEM["pe"], mget("pe", ("cbc", s, e - 1, w)))
                        y.wait_ge(SEM["st"], mget("st", ("ace", s, NST - 1)))
                        return y.dma_start(ce_row[w][:], CED[e:e + 1, s * S + w * W:s * S + (w + 1) * W])
                    emit(Y, yce, "ds", ("cce", s, e, w), dma=True)
                    def tbc(t, s=s, e=e, w=w):
                        t.wait_ge(SEM["ds"], mget("ds", ("cce", s, e, w)))
                        if (e, w) != (0, 0):
                            pe_, pw = (e, w - 1) if w else (e - 1, NW - 1)
                            t.wait_ge(SEM["ve"], mget("ve", ("ctmp", s, pe_, pw, 6)))
                        t.matmul(bank(4), ones1f[:], ce_row[w][0:1, 0:512], start=True, stop=True)
                        return t.matmul(bank(5), ones1f[:], ce_row[w][0:1, 512:1024], start=True, stop=True)
                    emit(T, tbc, "pe", ("cbc", s, e, w))
                    def vbc(v, s=s, e=e, w=w):
                        v.wait_ge(SEM["pe"], mget("pe", ("cbc", s, e, w)))
                        v.tensor_copy(bc_ce[w][:, 0:512], bank(4))
                        return v.tensor_copy(bc_ce[w][:, 512:1024], bank(5))
                    emit(V, vbc, "ve", ("cbce", s, e, w))
                    for hh in range(2):
                        for hcq in range(8):
                            hc = hh * 8 + hcq
                            def tg(t, s=s, e=e, w=w, hh=hh, hcq=hcq, hc=hc):
                                if (hcq, w) == (0, 0):
                                    t.wait_ge(SEM["w"], mget("w", ("cw", s, e, hh)))
                                if (hh, hcq) == (0, 0):
                                    if (e, w) != (0, 0):
                                        pe_, pw = (e, w - 1) if w else (e - 1, NW - 1)
                                        t.wait_ge(SEM["ve"], mget("ve", ("ctmp", s, pe_, pw, NC8 - 1)))
                                pv = c_prev_hc(s, e, w, hh, hcq)
                                if pv is not None:
                                    t.wait_ge(SEM["sc"], mget("sc", ("cgel",) + pv))
                                last = None
                                for dc in range(NC8):
                                    for ts2 in range(2):
                                        last = t.matmul(bank(ts2),
                                                        g0v(hh, dc, hcq),
                                                        xnb[:, dc * S + w * W + ts2 * 512:dc * S + w * W + (ts2 + 1) * 512],
                                                        start=(dc == 0), stop=(dc == NC8 - 1))
                                last.then_inc(SEM["pe"], 1)
                                pv2 = c_prev_hc(s, e, w, hh, hcq)
                                if pv2 is not None:
                                    t.wait_ge(SEM["ve"], mget("ve", ("cact",) + pv2))
                                for dc in range(NC8):
                                    for ts2 in range(2):
                                        last = t.matmul(bank(2 + ts2),
                                                        g1v(hh, dc, hcq),
                                                        xnb[:, dc * S + w * W + ts2 * 512:dc * S + w * W + (ts2 + 1) * 512],
                                                        start=(dc == 0), stop=(dc == NC8 - 1))
                                return last
                            mark("pe", ("cg0", s, e, w, hh, hcq))
                            emit(T, tg, "pe", ("cg", s, e, w, hh, hcq))
                            def sg(s_, s=s, e=e, w=w, hc=hc, hh=hh, hcq=hcq):
                                s_.wait_ge(SEM["pe"], mget("pe", ("cg0", s, e, w, hh, hcq)))
                                pv = c_prev_hc(s, e, w, hh, hcq)
                                if pv is not None:
                                    s_.wait_ge(SEM["ve"], mget("ve", ("cact",) + pv))
                                return s_.activation(geluv[:], PS[:, 0:1024], AF.Gelu_apprx_tanh)
                            emit(SS_, sg, "sc", ("cgel", s, e, w, hc))
                            def vg(v, s=s, e=e, w=w, hc=hc):
                                v.wait_ge(SEM["pe"], mget("pe", ("cg", s, e, w, hc // 8, hc % 8)))
                                v.wait_ge(SEM["sc"], mget("sc", ("cgel", s, e, w, hc)))
                                return v.tensor_tensor(act[hc][:], geluv[:], PS[:, 1024:2048], ALU.mult)
                            emit(V, vg, "ve", ("cact", s, e, w, hc))
                    for oc in range(NC8):
                        yb = 4 if oc % 2 == 0 else 0
                        def td(t, s=s, e=e, w=w, oc=oc, yb=yb):
                            if oc == 0:
                                t.wait_ge(SEM["ve"], mget("ve", ("cact", s, e, w, NHC - 1)))
                                t.wait_ge(SEM["w"], mget("w", ("cl", s, e)))
                                t.wait_ge(SEM["ve"], mget("ve", ("cbce", s, e, w)))
                            if oc == 1:
                                pass
                            if oc >= 2:
                                t.wait_ge(SEM["ve"], mget("ve", ("ctmp", s, e, w, oc - 2)))
                            last = None
                            for hc in range(NHC):
                                for ts2 in range(2):
                                    last = t.matmul(bank(yb + ts2), lnv(hc, oc),
                                                    act[hc][:, ts2 * 512:(ts2 + 1) * 512],
                                                    start=(hc == 0), stop=(hc == NHC - 1))
                            return last
                        emit(T, td, "pe", ("cd", s, e, w, oc))
                        def vd(v, s=s, e=e, w=w, oc=oc, yb=yb):
                            v.wait_ge(SEM["pe"], mget("pe", ("cd", s, e, w, oc)))
                            idx = oc
                            if idx >= 2:
                                v.wait_ge(SEM["acc"], mget("acc", ("cacc", s, e, w, oc - 2)))
                            elif (e, w) != (0, 0):
                                pe_, pw = (e, w - 1) if w else (e - 1, NW - 1)
                                v.wait_ge(SEM["acc"], mget("acc", ("cacc", s, pe_, pw, NC8 - 2 + oc)))
                            return v.tensor_tensor(tmpc[oc % 2][:], PS[:, yb * 512:yb * 512 + 1024],
                                                   bc_ce[w][:], ALU.mult)
                        emit(V, vd, "ve", ("ctmp", s, e, w, oc))
                        def gd(g, s=s, e=e, w=w, oc=oc):
                            g.wait_ge(SEM["ve"], mget("ve", ("ctmp", s, e, w, oc)))
                            if e == 0:
                                g.wait_ge(SEM["st"], mget("st", ("bst", s, 3, NC8 - 1)))
                            return g.dma_start(ACC[oc * 128:(oc + 1) * 128,
                                                   s * S + w * W:s * S + (w + 1) * W],
                                               tmpc[oc % 2][:],
                                               accum_op=ALU.add)
                        emit(G, gd, "acc", ("cacc", s, e, w, oc), dma=True)

        def phase_f(s):
            for st in range(NST):
                def yf(y, s=s, st=st):
                    if st == 0:
                        y.wait_ge(SEM["acc"], mget("acc", ("cacc", s, E - 1, NW - 1, NC8 - 1)))
                    if st >= 2:
                        y.wait_ge(SEM["pe"], mget("pe", ("fT", s, st - 2)))
                        y.wait_ge(SEM["ve"], mget("ve", ("fev", s, st - 2)))
                    src = bass.AP(ACC.tensor, s * S + st * 128,
                                  [[TOK, 128], [128 * TOK, 8], [1, 128]])
                    y.dma_start(accf[st % 2][:].rearrange("p (c t) -> p c t", c=8), src
                                ).then_inc(SEM["ds"], 16)
                    return y.dma_start(x_tok[st % 2][:], XB[s * S + st * 128:s * S + (st + 1) * 128, :])
                premark("ds", ("fla", s, st))
                emit(Y, yf, "ds", ("fl", s, st), dma=True)
                def tf(t, s=s, st=st):
                    t.wait_ge(SEM["ds"], mget("ds", ("fl", s, st)))
                    if st >= 1:
                        t.wait_ge(SEM["ve"], mget("ve", ("fev", s, st - 1)))
                    last = None
                    for oc in range(NC8):
                        last = t.transpose(PS[:, (oc // 4) * 512 + (oc % 4) * 128:(oc // 4) * 512 + (oc % 4) * 128 + 128],
                                           accf[st % 2][:, oc * 128:(oc + 1) * 128], ident32[:])
                    return last
                emit(T, tf, "pe", ("fT", s, st))
                def vf(v, s=s, st=st):
                    v.wait_ge(SEM["pe"], mget("pe", ("fT", s, st)))
                    if st >= 2:
                        v.wait_ge(SEM["os"], mget("os", ("fo", s, st - 2)))
                    last = None
                    for oc in range(NC8):
                        last = v.tensor_tensor(out_sb[st % 2][:, oc * 128:(oc + 1) * 128],
                                               PS[:, (oc // 4) * 512 + (oc % 4) * 128:(oc // 4) * 512 + (oc % 4) * 128 + 128],
                                               x_tok[st % 2][:, oc * 128:(oc + 1) * 128], ALU.add)
                    return last
                emit(V, vf, "ve", ("fev", s, st))
                def yf2(y, s=s, st=st):
                    y.wait_ge(SEM["ve"], mget("ve", ("fev", s, st)))
                    return y.dma_start(OUT[s * S + st * 128:s * S + (st + 1) * 128, :], out_sb[st % 2][:])
                emit(Y, yf2, "os", ("fo", s, st), dma=True)

        ph = DBG_PHASES if DBG_PHASES is not None else {"a", "b", "c", "f"}
        nseq = DBG_NSEQ if DBG_NSEQ is not None else NSEQ
        prep()
        join_all()
        for s in range(nseq):
            if "a" in ph:
                phase_a(s)
                join_all()
                if DBG_DUMP and s == 0:
                    def ydump(y):
                        return y.dma_start(XND, xnb[:])
                    emit(Y, ydump, "os", ("xnd",), dma=True)
                    join_all()
            if "b" in ph:
                phase_b(s)
                join_all()
            if "c" in ph:
                phase_c(s)
                join_all()
            if "f" in ph:
                phase_f(s)
                join_all()
        os_total = n["os"]
        if missing_keys:
            import collections
            cnt = collections.Counter(k[0] + ":" + str(k[1][0]) for k in missing_keys)
            print("DBG missing milestone keys (treated as 0):", dict(cnt))
        if os_total:
            Y.append(lambda y: y.wait_ge(SEM["os"], os_total))
        fin = dict(n)
        for snm in ("ds", "st", "w", "acc"):
            if fin[snm]:
                Y.append(lambda y, snm=snm, v=fin[snm]: y.wait_ge(SEM[snm], v))

        with nc.Block() as block:
            @block.tensor
            def _(t):
                for f in T:
                    f(t)

            @block.scalar
            def _(s_):
                for f in SS_:
                    f(s_)

            @block.vector
            def _(v):
                for f in V:
                    f(v)

            @block.gpsimd
            def _(g):
                for f in G:
                    f(g)

            @block.sync
            def _(y):
                for f in Y:
                    f(y)

    return nc


def kernel(L0_attn_x, L0_attn_w_q, L0_attn_w_k, L0_attn_w_v, L0_attn_w_out,
           L0_attn_softmax_scale, L0_ffn_router_scale, L0_ffn_router_w,
           L0_ffn_gating_w, L0_ffn_linear_w, L0_ffn_per_expert_scale):
    x = np.ascontiguousarray(np.asarray(L0_attn_x, dtype=np.float32))
    wq = np.ascontiguousarray(np.asarray(L0_attn_w_q, dtype=np.float32).T)
    wkv = np.ascontiguousarray(
        np.concatenate([np.asarray(L0_attn_w_k, dtype=np.float32).T,
                        np.asarray(L0_attn_w_v, dtype=np.float32).T], axis=1))
    wo = np.ascontiguousarray(np.asarray(L0_attn_w_out, dtype=np.float32).T)
    ss = np.asarray(L0_attn_softmax_scale, dtype=np.float32).reshape(1, 1)
    rw = np.ascontiguousarray(np.asarray(L0_ffn_router_w, dtype=np.float32))
    rsc = np.ascontiguousarray(np.asarray(L0_ffn_router_scale, dtype=np.float32).reshape(8, 128))
    pes = np.ascontiguousarray(np.asarray(L0_ffn_per_expert_scale, dtype=np.float32).reshape(1, 8))
    g0 = np.ascontiguousarray(np.asarray(L0_ffn_gating_w, dtype=np.float32)[:, 0].transpose(0, 2, 1))
    g1 = np.ascontiguousarray(np.asarray(L0_ffn_gating_w, dtype=np.float32)[:, 1].transpose(0, 2, 1))
    lin = np.ascontiguousarray(np.asarray(L0_ffn_linear_w, dtype=np.float32))

    nc = build_program()
    in_maps = []
    for c in range(8):
        xb = np.ascontiguousarray(x[4 * c:4 * (c + 1)].reshape(TOK, D))
        in_maps.append({"XB": xb, "WQ": wq, "WKV": wkv, "WO": wo, "SSD": ss,
                        "RW": rw, "RSC": rsc, "PESD": pes,
                        "G0D": g0, "G1D": g1, "LND": lin})
    res = run_bass_kernel_spmd(nc, in_maps, core_ids=list(range(8)))
    out = np.concatenate([r["OUT"].reshape(NSEQ, S, D) for r in res.results], axis=0)
    return out
